# version 01 (final)
# baseline (speedup 1.0000x reference)
import contextlib
import numpy as np
import ml_dtypes
import concourse.bass as bass
import concourse.mybir as mybir
from concourse.bass_utils import run_bass_kernel_spmd

F32 = mybir.dt.float32
BF16 = mybir.dt.bfloat16
AF = mybir.ActivationFunctionType
ALU = mybir.AluOpType
AX = mybir.AxisListType.X

NCORES = 8
T = 2048
NT = 16
D = 1024
NC = 8
LN_EPS = 1e-5
TTT_LR = 0.01
DEBUG = False


_ENGINE_ATTR = {"PE": "tensor", "DVE": "vector", "ACT": "scalar",
                "SP": "sync", "POOL": "gpsimd"}


def _sem_of(stream, kind):
    if kind == "c":
        return stream
    if kind == "d":
        return stream + "q"
    return "CC"


_ALL_SEMS = ["PE", "DVE", "ACT", "SP", "POOL", "SPq", "ACTq", "POOLq", "CC"]
_INC = {"c": 1, "d": 16, "cc": 1}


class Rec:
    def __init__(self):
        self.ops = []
        self.writers = {}
        self.readers = {}
        self.stream_last = {s: None for s in _ENGINE_ATTR}
        self._barrier_for = {s: set() for s in _ENGINE_ATTR}
        self._alias = {}

    def alias(self, new_keys, old_keys):
        for k in new_keys:
            self._alias[k] = list(old_keys)

    def op(self, stream, kind, fn, reads=(), writes=(), tag=""):
        oid = len(self.ops)
        writes = list(writes)
        for k in list(writes):
            if k in self._alias:
                writes.extend(self._alias.pop(k))
        deps = set()
        for k in reads:
            w = self.writers.get(k)
            if w is not None:
                deps.add(w)
            self.readers.setdefault(k, []).append(oid)
        for k in writes:
            w = self.writers.get(k)
            if w is not None:
                deps.add(w)
            for r in self.readers.get(k, ()):
                deps.add(r)
            self.writers[k] = oid
            self.readers[k] = []
        deps |= self._barrier_for[stream]
        self._barrier_for[stream] = set()
        deps.discard(oid)
        kept = set()
        for d in deps:
            od = self.ops[d]
            if (od["stream"] == stream and od["kind"] == "c" and kind == "c"
                    and stream == "PE"):
                continue
            kept.add(d)
        self.ops.append(dict(stream=stream, kind=kind, fn=fn, deps=kept,
                             need_inc=False, cum=0, tag=tag))
        self.stream_last[stream] = oid
        return oid

    def barrier(self):
        lasts = {l for l in self.stream_last.values() if l is not None}
        for s in _ENGINE_ATTR:
            self._barrier_for[s] |= lasts

    def emit(self, nc, sems):
        for op in self.ops:
            if op["kind"] in ("d", "cc") and op["fn"] is not None:
                op["need_inc"] = True
            for d in op["deps"]:
                self.ops[d]["need_inc"] = True
        cnt = {s: 0 for s in _ALL_SEMS}
        for op in self.ops:
            sem = _sem_of(op["stream"], op["kind"])
            op["sem"] = sem
            if op["need_inc"]:
                cnt[sem] += _INC[op["kind"]]
            op["cum"] = cnt[sem]

        clock = {s: {} for s in _ENGINE_ATTR}
        per_stream = {s: [] for s in _ENGINE_ATTR}
        n_waits = 0
        for op in self.ops:
            s = op["stream"]
            req = {}
            for d in op["deps"]:
                od = self.ops[d]
                for sem, v in od["vc"].items():
                    if req.get(sem, 0) < v:
                        req[sem] = v
                if req.get(od["sem"], 0) < od["cum"]:
                    req[od["sem"]] = od["cum"]
            waits = []
            for sem, v in sorted(req.items()):
                if clock[s].get(sem, 0) < v:
                    waits.append((sem, v))
                    clock[s][sem] = v
            n_waits += len(waits)
            op["waits"] = waits
            op["vc"] = dict(clock[s])
            per_stream[s].append(op)

        with nc.Block() as block:
            for s in _ENGINE_ATTR:
                ops = per_stream[s]
                if not ops:
                    continue
                section = getattr(block, _ENGINE_ATTR[s])

                def make_body(ops=ops):
                    def body(eng):
                        for op in ops:
                            for sem, v in op["waits"]:
                                eng.wait_ge(sems[sem], v)
                            inst = op["fn"](eng)
                            if op["need_inc"]:
                                if isinstance(inst, (list, tuple)):
                                    inst = inst[-1]
                                assert inst is not None, f"op {op['tag']} needs inc"
                                inst.then_inc(sems[op["sem"]], _INC[op["kind"]])
                    return body

                section(make_body())
        return dict(n_ops=len(self.ops), n_waits=n_waits)


class Arena:

    def __init__(self, ap, ncols):
        self.ap = ap
        self.ncols = ncols
        self.off = 0
        self.stack = []
        self.hwm = 0

    def alloc_f32(self, cols, name=""):
        assert self.off + cols <= self.ncols, \
            f"arena overflow: {name} needs {cols}, free {self.ncols - self.off}"
        ap = self.ap[:, self.off:self.off + cols]
        self.off += cols
        self.hwm = max(self.hwm, self.off)
        return ap

    def alloc_bf16(self, cols, name=""):
        return self.alloc_f32((cols + 1) // 2, name).bitcast(BF16)

    def push(self):
        self.stack.append(self.off)

    def pop(self):
        self.off = self.stack.pop()


class Builder:
    def __init__(self, name, sublayers, x_src="x_in", x_dst="x_out"):
        self.name = name
        self.sublayers = sublayers
        self.x_src = x_src
        self.x_dst = x_dst
        self.input_specs = {}

    def inp(self, name, shape, dtype, fn):
        if name not in self.input_specs:
            self.input_specs[name] = fn
            self.dram[name] = self.nc.dram_tensor(name, list(shape), dtype,
                                                  kind="ExternalInput").ap()
        return self.dram[name]

    def internal(self, name, shape, dtype):
        if name not in self.dram:
            self.dram[name] = self.nc.dram_tensor(name, list(shape), dtype).ap()
        return self.dram[name]

    def output(self, name, shape, dtype):
        if name not in self.dram:
            self.dram[name] = self.nc.dram_tensor(name, list(shape), dtype,
                                                  kind="ExternalOutput").ap()
        return self.dram[name]

    def build(self, weights):
        self.w = weights
        nc = bass.Bass("TRN2", target_bir_lowering=False)
        self.nc = nc
        self.dram = {}
        self.rec = Rec()
        self._uid = 0

        with contextlib.ExitStack() as st:
            sems = {s: st.enter_context(nc.semaphore("sem_" + s)) for s in _ALL_SEMS}
            self.ps = [st.enter_context(nc.psum_tensor(f"ps{i}", [128, 512], F32))
                       for i in range(6)]
            self.psb = [st.enter_context(nc.psum_tensor(f"psb{i}", [128, 1024], BF16))
                        for i in range(2)]
            self._ps_rr = 0
            self._psb_rr = 0

            self.sb = {}

            def sbuf(name, shape, dtype):
                t = st.enter_context(nc.sbuf_tensor(name, list(shape), dtype))
                self.sb[name] = t
                return t

            arena_cols = 34600
            self.arena = Arena(sbuf("arena", [128, arena_cols], F32)[:, :], arena_cols)
            self.x_h = sbuf("x_sb", [128, NT * D], F32)
            self.x_sb = self.x_h[:, :]
            sbuf("consts", [128, 128 * 6], F32)
            sbuf("identb", [128, 128], BF16)
            sbuf("onesrow", [1, 128], F32)
            sbuf("stats", [128, 128], F32)
            sbuf("smalls", [128, 64], F32)
            sbuf("dump", [128, 1024], F32)

            self._declare_common_inputs()
            self._load_prologue()
            for i, sl in enumerate(self.sublayers):
                self.arena.push()
                getattr(self, "emit_" + sl)(f"s{i}")
                self.arena.pop()
                self.rec.barrier()
            self._store_epilogue()
            stats = self.rec.emit(nc, sems)
            stats["arena_hwm_cols"] = self.arena.hwm
            self.stats = stats
        return nc

    def uid(self):
        self._uid += 1
        return self._uid

    def ps_bank(self):
        i = self._ps_rr % 6
        self._ps_rr += 1
        return self.ps[i][:, :], f"ps{i}"

    def psb_bank(self):
        i = self._psb_rr % 2
        self._psb_rr += 1
        return self.psb[i][:, :], f"psb{i}"

    def mm_group(self, psum_ap, psum_key, pairs, reads, tag=""):
        def fn(eng, pairs=pairs, psum_ap=psum_ap):
            inst = None
            n = len(pairs)
            for k, (l, r) in enumerate(pairs):
                inst = eng.matmul(psum_ap, l, r, start=(k == 0), stop=(k == n - 1))
            return inst
        return self.rec.op("PE", "c", fn, reads=list(reads), writes=[psum_key], tag=tag)

    def transpose(self, out_psum, psum_key, in_ap, reads, bf16=True, tag="tr"):
        p = in_ap.shape[0]
        ident = (self.sb["identb"][0:p, 0:p] if bf16
                 else self.sb["consts"][0:p, 0:p])
        def fn(eng, o=out_psum, i=in_ap, id=ident):
            return eng.transpose(o, i, id)
        return self.rec.op("PE", "c", fn, reads=list(reads) + ["consts", "identb"],
                           writes=[psum_key], tag=tag)

    def dve(self, fn, reads, writes, tag=""):
        return self.rec.op("DVE", "c", fn, list(reads), list(writes), tag)

    def act(self, fn, reads, writes, tag=""):
        return self.rec.op("ACT", "c", fn, list(reads), list(writes), tag)

    def pe(self, fn, reads, writes, tag=""):
        return self.rec.op("PE", "c", fn, list(reads), list(writes), tag)

    def dma(self, q, out, in_, reads, writes, tag=""):
        def fn(eng, out=out, in_=in_):
            return eng.dma_start(out=out, in_=in_)
        return self.rec.op(q, "d", fn, list(reads), list(writes), tag)

    def dma_t(self, q, out, in_, reads, writes, tag=""):
        def fn(eng, out=out, in_=in_):
            return eng.dma_start_transpose(out, in_)
        return self.rec.op(q, "d", fn, list(reads), list(writes), tag)

    def collective(self, ins_ap, outs_ap, reads, writes, tag=""):
        def fn(eng, i=ins_ap, o=outs_ap):
            return eng.collective_compute(
                "AllGather", ALU.bypass, ins=[i], outs=[o],
                replica_groups=[[0, 1, 2, 3], [4, 5, 6, 7]])
        return self.rec.op("POOL", "cc", fn, list(reads), list(writes), tag)

    def _declare_common_inputs(self):
        ident = np.eye(128, dtype=np.float32)
        consts = np.concatenate([
            ident, 2 * ident,
            TTT_LR * np.tril(np.ones((128, 128), np.float32), -1),
            -TTT_LR * np.triu(np.ones((128, 128), np.float32), 1),
            TTT_LR * np.triu(np.ones((128, 128), np.float32), 1),
            np.where(np.tril(np.ones((128, 128), bool)), 0.0, -1e30).astype(np.float32),
        ], 1)
        self.inp("c_consts", [128, 128 * 6], F32, lambda c: consts)
        self.inp("c_identb", [128, 128], BF16, lambda c: ident.astype(ml_dtypes.bfloat16))
        self.inp("c_ones", [1, 128], F32, lambda c: np.ones((1, 128), np.float32))

        def smalls_fn(c):
            rl = c % 4
            out = np.zeros((128, 64), np.float32)
            if rl > 0:
                out[:, rl - 1] = 1.0
            out[:, 4:4 + rl] = 1.0
            return out
        self.inp("c_smalls", [128, 64], F32, smalls_fn)

    def _load_prologue(self):
        x_dram = self.inp(self.x_src, [T, D], F32, lambda c: None)
        self.dma("SP", self.x_h[:, :].rearrange("p (t d) -> p t d", t=NT),
                 x_dram.rearrange("(t p) d -> p t d", p=128),
                 reads=[], writes=["x:ALL"] + [f"x:{i}" for i in range(NT)], tag="xload")
        for src, dst in [("c_consts", "consts"), ("c_identb", "identb"),
                         ("c_ones", "onesrow"), ("c_smalls", "smalls")]:
            self.dma("SP", self.sb[dst][:, :], self.dram[src],
                     reads=[], writes=[dst], tag="cload")

    def _store_epilogue(self):
        out = self.output(self.x_dst, [T, D], F32)
        self.dma("SP", out.rearrange("(t p) d -> p t d", p=128),
                 self.x_h[:, :].rearrange("p (t d) -> p t d", t=NT),
                 reads=["x:ALL"] + [f"x:{i}" for i in range(NT)],
                 writes=["xout"], tag="xstore")
        self.rec.op("SP", "c", lambda eng: None, ["xout"], [], "final")

    def dbg(self, name, ap, shape, reads, dtype=F32):
        if not DEBUG:
            return
        o = self.output("dbg_" + name, shape, dtype)
        self.dma("ACT", o, ap, reads=list(reads), writes=[], tag="dbg_" + name)

    def emit_ln(self, z_cb, tag, post_cb=None):
        B = self
        stats = B.sb["stats"][:, :]
        sums, sumsq = stats[:, 0:16], stats[:, 16:32]
        mean, m2 = stats[:, 32:48], stats[:, 48:64]
        var, stdv = stats[:, 64:80], stats[:, 80:96]
        rstd, negmean = stats[:, 96:112], stats[:, 112:128]
        dump = B.sb["dump"][:, :]
        for i in range(NT):
            xt = B.x_sb[:, i * D:(i + 1) * D]
            B.dve(lambda e, xt=xt, o=sums[:, i:i + 1]:
                  e.reduce_sum(out=o, in_=xt, axis=AX),
                  [f"x:{i}"], [f"st_s{i}"], f"{tag}:sum{i}")
            B.act(lambda e, xt=xt, o=sumsq[:, i:i + 1]:
                  e.activation(out=dump, in_=xt, func=AF.Square, accum_out=o),
                  [f"x:{i}"], [f"st_q{i}", "dump"], f"{tag}:ssq{i}")
        rs_s = [f"st_s{i}" for i in range(NT)]
        rs_q = [f"st_q{i}" for i in range(NT)]
        B.dve(lambda e: e.tensor_scalar_mul(out=mean, in0=sums, scalar1=1.0 / D),
              rs_s, ["st_mean"], f"{tag}:mean")
        B.dve(lambda e: e.tensor_tensor(out=m2, in0=mean, in1=mean, op=ALU.mult),
              ["st_mean"], ["st_m2"], f"{tag}:m2")
        B.dve(lambda e: e.scalar_tensor_tensor(out=var, in0=sumsq, scalar=1.0 / D,
                                               in1=m2, op0=ALU.mult, op1=ALU.subtract),
              rs_q + ["st_m2"], ["st_var"], f"{tag}:var")
        B.dve(lambda e: e.tensor_scalar_add(out=var, in0=var, scalar1=LN_EPS),
              ["st_var"], ["st_var"], f"{tag}:eps")
        B.act(lambda e: e.activation(out=stdv, in_=var, func=AF.Sqrt),
              ["st_var"], ["st_std"], f"{tag}:std")
        B.dve(lambda e: e.reciprocal(out=rstd, in_=stdv),
              ["st_std"], ["st_rstd"], f"{tag}:rstd")
        B.dve(lambda e: e.tensor_scalar_mul(out=negmean, in0=mean, scalar1=-1.0),
              ["st_mean"], ["st_negmean"], f"{tag}:negmean")
        for i in range(NT):
            xt = B.x_sb[:, i * D:(i + 1) * D]
            dst, dkey = z_cb(i)
            B.dve(lambda e, xt=xt, dst=dst, i=i:
                  e.tensor_scalar(out=dst, in0=xt, scalar1=negmean[:, i:i + 1],
                                  scalar2=rstd[:, i:i + 1], op0=ALU.add, op1=ALU.mult),
                  [f"x:{i}", "st_negmean", "st_rstd"], [dkey], f"{tag}:z{i}")
            if post_cb is not None:
                post_cb(i)

    def emit_halo(self, zfm, zkey, tag):
        B = self
        u = self.uid()
        hc_in = B.internal(f"hcin{u}", [128, 8], F32)
        hc_out = B.internal(f"hcout{u}", [512, 8], F32)
        hsb = B.arena.alloc_f32(8, "hsb")
        gsb = B.arena.alloc_f32(32, "gsb")
        halo = B.arena.alloc_f32(8, "halo")
        hkey, gkey, akey = f"hsb{u}", f"gsb{u}", f"halo{u}"
        for k in range(NC):
            B.dve(lambda e, k=k: e.tensor_copy(
                out=hsb[:, k:k + 1], in_=zfm[:, k * 2049 + 2048:k * 2049 + 2049]),
                  [zkey], [hkey], f"{tag}:hx{k}")
        B.dma("SP", hc_in, hsb, [hkey], [f"hcin{u}"], f"{tag}:hout")
        B.collective(hc_in, hc_out, [f"hcin{u}"], [f"hcout{u}"], f"{tag}:ag")
        for r in range(4):
            B.dma("SP", gsb[:, r * 8:(r + 1) * 8], hc_out[r * 128:(r + 1) * 128, :],
                  [f"hcout{u}"], [gkey], f"{tag}:hin{r}")
        self.dbg(f"halog{u}", gsb, [128, 32], [gkey])
        self.dbg(f"halos{u}", hsb, [128, 8], [hkey])
        hm = B.sb["smalls"]
        B.act(lambda e: e.activation(out=halo, in_=gsb[:, 0:8], func=AF.Copy,
                                     scale=hm[:, 0:1]),
              [gkey, "smalls"], [akey], f"{tag}:hm0")
        for r in range(1, 4):
            B.dve(lambda e, r=r: e.scalar_tensor_tensor(
                out=halo, in0=gsb[:, r * 8:(r + 1) * 8], scalar=hm[:, r:r + 1],
                in1=halo, op0=ALU.mult, op1=ALU.add),
                  [gkey, "smalls", akey], [akey], f"{tag}:hm{r}")
        for k in range(NC):
            B.dve(lambda e, k=k: e.tensor_copy(out=zfm[:, k * 2049:k * 2049 + 1],
                                               in_=halo[:, k:k + 1]),
                  [akey], [zkey], f"{tag}:hw{k}")

    def load_w_bf16(self, dram_name, rows, cols, arr_fn, tag=""):
        B = self
        dr = B.inp(dram_name, [rows, cols], F32, arr_fn)
        nchunk = rows // 128
        dst = B.arena.alloc_bf16(nchunk * cols, dram_name)
        key = f"w{dram_name}_{self.uid()}"
        B.dma("POOL", dst.rearrange("p (k c) -> p k c", k=nchunk),
              dr.rearrange("(k p) c -> p k c", p=128),
              [], [key], f"{tag}:ld_{dram_name}")
        return dst, key

    def emit_l0ffn(self, tag):
        B = self
        w = B.w
        ar = B.arena
        mus_in = B.inp("c_mus", [128, 16], F32, lambda c: np.concatenate(
            [np.ascontiguousarray(w["L0_ffn_mu_r"].reshape(8, 128).T),
             np.ascontiguousarray(w["L0_ffn_mu_k"].reshape(8, 128).T)], 1))
        musb = ar.alloc_f32(16, "musb")
        onem = ar.alloc_f32(16, "onem")
        B.dma("SP", musb, mus_in, [], [f"{tag}mu"], f"{tag}:muld")
        B.dve(lambda e: e.tensor_scalar(out=onem, in0=musb, scalar1=-1.0, scalar2=1.0,
                                        op0=ALU.mult, op1=ALU.add),
              [f"{tag}mu"], [f"{tag}onem"], f"{tag}:onem")

        slotA = ar.alloc_f32(8196, "slotA")
        slotB = ar.alloc_f32(8192, "slotB")
        slotC = ar.alloc_f32(8192, "slotC")
        slotW1 = ar.alloc_f32(4096, "slotW1")
        slotW2 = ar.alloc_f32(4096, "slotW2")
        ztmp = ar.alloc_f32(1024, "ztmp")

        zfm = slotA.bitcast(BF16)
        zt = [ztmp[:, 0:512].bitcast(BF16), ztmp[:, 512:1024].bitcast(BF16)]
        zkey = f"{tag}zfm"

        def zcb(i):
            return zt[i % 2], f"{tag}zt{i % 2}"

        def post(i):
            for k in range(NC):
                pb, pbk = B.psb_bank()
                B.transpose(pb[:, 0:128], pbk, zt[i % 2][:, k * 128:(k + 1) * 128],
                            [f"{tag}zt{i % 2}"], tag=f"{tag}:tr{i}.{k}")
                dst = zfm[:, k * 2049 + 1 + i * 128:k * 2049 + 1 + (i + 1) * 128]
                B.act(lambda e, p=pb, dst=dst: e.copy(out=dst, in_=p[:, 0:128]),
                      [pbk], [zkey], f"{tag}:te{i}.{k}")
        self.emit_ln(zcb, f"{tag}ln", post_cb=post)
        self.dbg(f"{tag}stats", B.sb["stats"][:, :], [128, 128],
                 ["st_negmean", "st_rstd", "st_var", "st_mean", "st_m2", "st_std"])
        self.emit_halo(zfm, zkey, tag)
        self.dbg(f"{tag}zfm", zfm, [128, NC * 2049], [zkey], BF16)

        wr = slotW1.bitcast(BF16)
        wk = slotW2.bitcast(BF16)
        wrk, wkk, wvk = f"{tag}wr", f"{tag}wk", f"{tag}wv"
        wr_dr = B.inp("wrT", [D, D], F32,
                      lambda c: np.ascontiguousarray(w["L0_ffn_W_r"].T))
        wk_dr = B.inp("wkT", [D, D], F32,
                      lambda c: np.ascontiguousarray(w["L0_ffn_W_k"].T))
        wv_dr = B.inp("wvT", [D, D], F32,
                      lambda c: np.ascontiguousarray(w["L0_ffn_W_v"].T))
        B.dma("POOL", wr.rearrange("p (k c) -> p k c", k=NC),
              wr_dr.rearrange("(k p) c -> p k c", p=128), [], [wrk], f"{tag}:ldwr")
        B.dma("POOL", wk.rearrange("p (k c) -> p k c", k=NC),
              wk_dr.rearrange("(k p) c -> p k c", p=128), [], [wkk], f"{tag}:ldwk")

        xr = slotB.bitcast(BF16)
        tmp = ztmp.bitcast(BF16)
        xr_keys = [f"{tag}xr:{k}" for k in range(NC)]
        xk_keys = [f"{tag}xk:{k}" for k in range(NC)]

        def blend(dst, keys, mcol_base, extra_first=()):
            for k in range(NC):
                zprev = zfm[:, k * 2049:k * 2049 + T]
                zcur = zfm[:, k * 2049 + 1:k * 2049 + 1 + T]
                B.act(lambda e, zp=zprev, k=k:
                      e.activation(out=tmp, in_=zp, func=AF.Copy,
                                   scale=onem[:, mcol_base + k:mcol_base + k + 1]),
                      [zkey, f"{tag}onem", f"{tag}zt0", f"{tag}zt1"],
                      [f"{tag}zt0", f"{tag}zt1"], f"{tag}:bl{mcol_base}.{k}")
                wl = [keys[k]] + (list(extra_first) if k == 0 else [])
                B.dve(lambda e, zc=zcur, k=k:
                      e.scalar_tensor_tensor(out=dst[:, k * T:(k + 1) * T], in0=zc,
                                             scalar=musb[:, mcol_base + k:mcol_base + k + 1],
                                             in1=tmp, op0=ALU.mult, op1=ALU.add),
                      [zkey, f"{tag}mu", f"{tag}zt0", f"{tag}zt1"], wl,
                      f"{tag}:bm{mcol_base}.{k}")

        blend(xr, xr_keys, 0, extra_first=[f"{tag}zt0", f"{tag}zt1"])
        self.dbg(f"{tag}xr", xr, [128, NC * T], xr_keys, BF16)

        rbuf = slotC.bitcast(BF16)
        for i in range(NT):
            for ds in range(2):
                pb, pbk = self.ps_bank()
                pairs = [(xr[:, k * T + i * 128:k * T + (i + 1) * 128],
                          wr[:, k * 1024 + ds * 512:k * 1024 + (ds + 1) * 512])
                         for k in range(NC)]
                self.mm_group(pb, pbk, pairs, xr_keys + [wrk], f"{tag}:rmm{i}.{ds}")
                B.act(lambda e, p=pb, i=i, ds=ds:
                      e.activation(out=rbuf[:, i * D + ds * 512:i * D + (ds + 1) * 512],
                                   in_=p, func=AF.Sigmoid),
                      [pbk], [f"{tag}r:{i}"], f"{tag}:rsig{i}.{ds}")

        self.dbg(f"{tag}rbuf", rbuf, [128, NT * D],
                 [f"{tag}r:{i}" for i in range(NT)], BF16)
        xk = slotB.bitcast(BF16)
        blend(xk, xk_keys, 8, extra_first=xr_keys)

        wv = slotW1.bitcast(BF16)
        B.dma("POOL", wv.rearrange("p (k c) -> p k c", k=NC),
              wv_dr.rearrange("(k p) c -> p k c", p=128), [], [wvk, wrk], f"{tag}:ldwv")

        kact = slotA[:, 0:8192].bitcast(BF16)
        krelu = B.sb["dump"][:, 0:512]
        kact_keys = [f"{tag}kact:{k}" for k in range(NC)]
        for dt in range(NC):
            for ts in range(4):
                pb, pbk = self.ps_bank()
                pairs = [(wk[:, k * 1024 + dt * 128:k * 1024 + (dt + 1) * 128],
                          xk[:, k * T + ts * 512:k * T + (ts + 1) * 512])
                         for k in range(NC)]
                self.mm_group(pb, pbk, pairs, xk_keys + [wkk], f"{tag}:kmm{dt}.{ts}")
                B.act(lambda e, p=pb: e.activation(out=krelu, in_=p, func=AF.Relu),
                      [pbk, "dump"], ["dump"], f"{tag}:krelu{dt}.{ts}")
                wl = [f"{tag}kact:{dt}"] + ([zkey] if dt == 0 and ts == 0 else [])
                B.act(lambda e, dt=dt, ts=ts: e.activation(
                    out=kact[:, dt * T + ts * 512:dt * T + (ts + 1) * 512],
                    in_=krelu, func=AF.Square),
                      ["dump"], wl, f"{tag}:ksq{dt}.{ts}")

        self.dbg(f"{tag}kact", kact, [128, NC * T], kact_keys, BF16)
        vtmp = B.sb["dump"][:, 512:1024]
        for i in range(NT):
            for ds in range(2):
                pb, pbk = self.ps_bank()
                pairs = [(kact[:, k * T + i * 128:k * T + (i + 1) * 128],
                          wv[:, k * 1024 + ds * 512:k * 1024 + (ds + 1) * 512])
                         for k in range(NC)]
                self.mm_group(pb, pbk, pairs, kact_keys + [wvk], f"{tag}:vmm{i}.{ds}")
                rv = rbuf[:, i * D + ds * 512:i * D + (ds + 1) * 512]
                B.dve(lambda e, p=pb, rv=rv: e.tensor_tensor(out=vtmp, in0=p, in1=rv,
                                                             op=ALU.mult),
                      [pbk, f"{tag}r:{i}", "dump2"], ["dump2"], f"{tag}:vr{i}.{ds}")
                xs = B.x_sb[:, i * D + ds * 512:i * D + (ds + 1) * 512]
                B.dve(lambda e, xs=xs: e.scalar_tensor_tensor(
                    out=xs, in0=vtmp, scalar=0.5, in1=xs, op0=ALU.mult, op1=ALU.add),
                      ["dump2", f"x:{i}"], [f"x:{i}"], f"{tag}:xacc{i}.{ds}")


    def bcast_row(self, row_dram, rowkey, dst, dkey, tag, n):
        B = self
        dump = B.sb["dump"]
        B.dma("SP", dump[0:1, 0:n], row_dram, ["dump"], ["dump"], f"{tag}:rowld")
        ones = B.sb["onesrow"][:, :]
        for j in range(0, n, 512):
            wd = min(512, n - j)
            pb, pbk = B.ps_bank()
            B.pe(lambda e, j=j, wd=wd, p=pb: e.matmul(p[:, 0:wd], ones,
                                                      dump[0:1, j:j + wd],
                                                      start=True, stop=True),
                 ["dump", "onesrow"], [pbk], f"{tag}:bc{j}")
            B.act(lambda e, j=j, wd=wd, p=pb: e.copy(out=dst[:, j:j + wd],
                                                     in_=p[:, 0:wd]),
                  [pbk], [dkey], f"{tag}:bce{j}")

    def emit_ttt(self, tag):
        B = self
        w = B.w
        ar = B.arena
        rec = B.rec
        DS = 64
        dump = B.sb["dump"][:, :]

        g_dr = B.inp("t_gamma", [1, D], F32, lambda c: w["L0_attn_gamma"].reshape(1, D))
        b_dr = B.inp("t_beta", [1, D], F32, lambda c: w["L0_attn_beta"].reshape(1, D))
        bkq_dr = B.inp("t_bkq", [1, 128], F32, lambda c: np.ascontiguousarray(
            np.concatenate([w["L0_attn_bk"], w["L0_attn_bq"]]).reshape(1, 128)))
        bv_dr = B.inp("t_bv", [1, D], F32, lambda c: w["L0_attn_bv"].reshape(1, D))
        bkqcol_dr = B.inp("t_bkqcol", [64, 2], F32, lambda c: np.ascontiguousarray(
            np.stack([w["L0_attn_bk"], w["L0_attn_bq"]], 1)))
        gamma_b = ar.alloc_f32(D, "gamma_b")
        beta_b = ar.alloc_f32(D, "beta_b")
        bkq_b = ar.alloc_f32(128, "bkq_b")
        bv_b = ar.alloc_f32(D, "bv_b")
        bkqcol = ar.alloc_f32(2, "bkqcol")
        self.bcast_row(g_dr, "g", gamma_b, f"{tag}gb", f"{tag}g", D)
        self.bcast_row(b_dr, "b", beta_b, f"{tag}bb", f"{tag}b", D)
        self.bcast_row(bkq_dr, "kq", bkq_b, f"{tag}kq", f"{tag}kq", 128)
        self.bcast_row(bv_dr, "vb", bv_b, f"{tag}vb", f"{tag}vbb", D)
        B.dma("SP", bkqcol[0:64, :], bkqcol_dr, [], [f"{tag}kqc"], f"{tag}:kqcld")

        wkq_dr = B.inp("t_wkq", [D, 128], F32, lambda c: np.ascontiguousarray(
            np.concatenate([w["L0_attn_Wk"].T, w["L0_attn_Wq"].T], 1)))
        wv_dr = B.inp("t_wv", [D, D], F32, lambda c: np.ascontiguousarray(
            w["L0_attn_Wv"].T))
        wkq = ar.alloc_bf16(NC * 128, "wkq")
        wv = ar.alloc_bf16(NC * D, "twv")
        B.dma("POOL", wkq.rearrange("p (k c) -> p k c", k=NC),
              wkq_dr.rearrange("(k p) c -> p k c", p=128), [], [f"{tag}wkq"],
              f"{tag}:ldkq")
        B.dma("POOL", wv.rearrange("p (k c) -> p k c", k=NC),
              wv_dr.rearrange("(k p) c -> p k c", p=128), [], [f"{tag}wv"],
              f"{tag}:ldwv")

        slotZ = ar.alloc_f32(2048, "slotZ")
        slotS = ar.alloc_f32(1536, "slotS")
        slotH = ar.alloc_f32(8192, "slotH")
        vtm = ar.alloc_bf16(NT * D, "vtm")
        wst = ar.alloc_f32(D, "wst")[0:64, :]
        w16 = ar.alloc_bf16(D, "w16")[0:64, :]
        gsl = ar.alloc_f32(2 * 1088, "gsl")[0:64, :]
        wcs = ar.alloc_f32(D, "wcs")[0:64, :]
        wcs16 = ar.alloc_bf16(D, "wcs16")[0:64, :]
        ctmp = ar.alloc_f32(512, "ctmp")[0:64, :]

        zbuf = [slotZ[:, 0:1024], slotZ[:, 1024:2048]]
        hscr = slotS[:, 0:1024]
        htile = slotS[:, 1024:1536].bitcast(BF16)
        hfm = slotH.bitcast(BF16)
        hkey = f"{tag}hfm"

        st2 = ar.alloc_f32(128, "st2")
        s2, q2 = st2[:, 0:16], st2[:, 16:32]
        mean2, m22 = st2[:, 32:48], st2[:, 48:64]
        var2, std2 = st2[:, 64:80], st2[:, 80:96]
        rstd2, negmean2 = st2[:, 96:112], st2[:, 112:128]

        def zcb(i):
            return zbuf[i % 2], f"{tag}zb{i % 2}"

        def post(i):
            zt = zbuf[i % 2]
            zk = f"{tag}zb{i % 2}"
            B.dve(lambda e, zt=zt, o=s2[:, i:i + 1]:
                  e.reduce_sum(out=o, in_=zt, axis=AX),
                  [zk], [f"{tag}s2_{i}"], f"{tag}:s2{i}")
            B.act(lambda e, zt=zt, o=q2[:, i:i + 1]:
                  e.activation(out=dump, in_=zt, func=AF.Square, accum_out=o),
                  [zk, "dump"], [f"{tag}q2_{i}", "dump"], f"{tag}:q2{i}")
        self.emit_ln(zcb, f"{tag}ln", post_cb=post)
        rs2 = [f"{tag}s2_{i}" for i in range(NT)]
        rq2 = [f"{tag}q2_{i}" for i in range(NT)]
        B.dve(lambda e: e.tensor_scalar_mul(out=mean2, in0=s2, scalar1=1.0 / D),
              rs2, [f"{tag}me2"], f"{tag}:me2")
        B.dve(lambda e: e.tensor_tensor(out=m22, in0=mean2, in1=mean2, op=ALU.mult),
              [f"{tag}me2"], [f"{tag}m22"], f"{tag}:m22")
        B.dve(lambda e: e.scalar_tensor_tensor(out=var2, in0=q2, scalar=1.0 / D,
                                               in1=m22, op0=ALU.mult, op1=ALU.subtract),
              rq2 + [f"{tag}m22"], [f"{tag}var2"], f"{tag}:var2")
        B.dve(lambda e: e.tensor_scalar_add(out=var2, in0=var2, scalar1=LN_EPS),
              [f"{tag}var2"], [f"{tag}var2"], f"{tag}:eps2")
        B.act(lambda e: e.activation(out=std2, in_=var2, func=AF.Sqrt),
              [f"{tag}var2"], [f"{tag}std2"], f"{tag}:std2")
        B.dve(lambda e: e.reciprocal(out=rstd2, in_=std2),
              [f"{tag}std2"], [f"{tag}rstd2"], f"{tag}:rstd2")
        B.dve(lambda e: e.tensor_scalar_mul(out=negmean2, in0=mean2, scalar1=-1.0),
              [f"{tag}me2"], [f"{tag}nm2"], f"{tag}:nm2")

        stats = B.sb["stats"][:, :]
        rstd, negmean = stats[:, 96:112], stats[:, 112:128]
        hsk, htk = f"{tag}hscr", f"{tag}ht"
        rec.alias([hsk], [f"{tag}zb0"])
        for i in range(NT):
            xt = B.x_sb[:, i * D:(i + 1) * D]
            B.dve(lambda e, xt=xt, i=i:
                  e.tensor_scalar(out=hscr, in0=xt, scalar1=negmean[:, i:i + 1],
                                  scalar2=rstd[:, i:i + 1], op0=ALU.add, op1=ALU.mult),
                  [f"x:{i}", "st_negmean", "st_rstd", hsk], [hsk], f"{tag}:z2r{i}")
            B.dve(lambda e, xt=xt: e.tensor_tensor(out=xt, in0=xt, in1=hscr,
                                                   op=ALU.add),
                  [hsk, f"x:{i}"], [f"x:{i}"], f"{tag}:xz{i}")
            B.dve(lambda e, i=i:
                  e.tensor_scalar(out=hscr, in0=hscr, scalar1=negmean2[:, i:i + 1],
                                  scalar2=rstd2[:, i:i + 1], op0=ALU.add, op1=ALU.mult),
                  [hsk, f"{tag}nm2", f"{tag}rstd2"], [hsk], f"{tag}:hn{i}")
            B.dve(lambda e: e.tensor_tensor(out=hscr, in0=hscr, in1=gamma_b,
                                            op=ALU.mult),
                  [hsk, f"{tag}gb"], [hsk], f"{tag}:hg{i}")
            B.dve(lambda e: e.tensor_tensor(out=htile, in0=hscr, in1=beta_b,
                                            op=ALU.add),
                  [hsk, f"{tag}bb", htk], [htk], f"{tag}:hb{i}")
            for k in range(NC):
                pb, pbk = B.psb_bank()
                B.transpose(pb[:, 0:128], pbk, htile[:, k * 128:(k + 1) * 128],
                            [htk], tag=f"{tag}:htr{i}.{k}")
                dst = hfm[:, k * T + i * 128:k * T + (i + 1) * 128]
                B.act(lambda e, p=pb, dst=dst: e.copy(out=dst, in_=p[:, 0:128]),
                      [pbk], [hkey], f"{tag}:hte{i}.{k}")

        kfm = slotZ[:, 0:1024][0:64, :].bitcast(BF16)
        qfm = slotZ[:, 1024:2048][0:64, :].bitcast(BF16)
        kfm_k, qfm_k = f"{tag}kfm", f"{tag}qfm"
        rec.alias([kfm_k], [f"{tag}zb0"])
        rec.alias([qfm_k], [f"{tag}zb1"])
        ktm = slotS[:, 0:512].bitcast(BF16)
        ktm_k = f"{tag}ktm"
        rec.alias([ktm_k], [hsk, htk])
        vtm_k = f"{tag}vtm"
        for ts in range(4):
            for which, dstb, dkey, off, bc in [("k", kfm, kfm_k, 0, 0),
                                               ("q", qfm, qfm_k, DS, 1)]:
                pb, pbk = B.ps_bank()
                pairs = [(wkq[:, k * 128 + off:k * 128 + off + DS],
                          hfm[:, k * T + ts * 512:k * T + (ts + 1) * 512])
                         for k in range(NC)]
                self.mm_group(pb[0:64, :], pbk, pairs, [f"{tag}wkq", hkey],
                              f"{tag}:{which}mm{ts}")
                B.dve(lambda e, p=pb, ts=ts, dstb=dstb, bc=bc:
                      e.tensor_scalar(out=dstb[:, ts * 512:(ts + 1) * 512],
                                      in0=p[0:64, :],
                                      scalar1=bkqcol[0:64, bc:bc + 1],
                                      scalar2=None, op0=ALU.add),
                      [pbk, f"{tag}kqc"], [dkey], f"{tag}:{which}b{ts}")
        for i in range(NT):
            pb, pbk = B.ps_bank()
            pairs = [(hfm[:, k * T + i * 128:k * T + (i + 1) * 128],
                      wkq[:, k * 128:k * 128 + DS]) for k in range(NC)]
            self.mm_group(pb[:, 0:DS], pbk, pairs, [f"{tag}wkq", hkey],
                          f"{tag}:ktm{i}")
            B.dve(lambda e, p=pb, i=i: e.tensor_tensor(
                out=ktm[:, i * DS:(i + 1) * DS], in0=p[:, 0:DS],
                in1=bkq_b[:, 0:DS], op=ALU.add),
                  [pbk, f"{tag}kq"], [ktm_k], f"{tag}:ktmb{i}")
            for ds in range(2):
                pb, pbk = B.ps_bank()
                pairs = [(hfm[:, k * T + i * 128:k * T + (i + 1) * 128],
                          wv[:, k * D + ds * 512:k * D + (ds + 1) * 512])
                         for k in range(NC)]
                self.mm_group(pb, pbk, pairs, [f"{tag}wv", hkey], f"{tag}:vmm{i}.{ds}")
                B.dve(lambda e, p=pb, i=i, ds=ds: e.tensor_tensor(
                    out=vtm[:, i * D + ds * 512:i * D + (ds + 1) * 512], in0=p,
                    in1=bv_b[:, ds * 512:(ds + 1) * 512], op=ALU.add),
                      [pbk, f"{tag}vbb"], [vtm_k], f"{tag}:vtmb{i}.{ds}")

        consts = B.sb["consts"]
        identf = consts[:, 0:128]
        c2i = consts[:, 128:256]
        lrtril = consts[:, 256:384]
        neglrtriu = consts[:, 384:512]
        lrtriu = consts[:, 512:640]
        minvt = slotH[:, 0:1024].bitcast(BF16)
        smt = slotH[:, 1024:2048].bitcast(BF16)
        ubuf = slotH[:, 2048:2560].bitcast(BF16)
        newt = [slotH[:, 2560 + j * 128:2560 + (j + 1) * 128] for j in range(6)]
        e16 = slotH[:, 3328:3840].bitcast(BF16)
        b16 = slotH[:, 3840:4352].bitcast(BF16)
        h16 = slotH[:, 4352:5376][0:64, :].bitcast(BF16)
        msb = slotH[:, 5760:6784]
        qt16 = slotH[:, 5504:5568][0:64, :].bitcast(BF16)
        rt = slotH[:, 5568:5632][0:64, :]
        rt16 = slotH[:, 5632:5664][0:64, :].bitcast(BF16)
        csb = slotH[:, 5696:5760][0:64, :]
        nk = lambda n: f"{tag}nw{n}"
        first_keys = ([nk(x) for x in "A At X T2 Xt scr".split()] +
                      [f"{tag}mv:{i}" for i in range(NT)] +
                      [f"{tag}sm:{i}" for i in range(NT)] +
                      [f"{tag}u:{i}" for i in range(NT)] +
                      [f"{tag}h16:{i}" for i in range(NT)] +
                      [f"{tag}M:{i}" for i in range(NT)] +
                      [f"{tag}e16", f"{tag}b16",
                       f"{tag}qt16", f"{tag}rt", f"{tag}rt16", f"{tag}csb"])
        rec.alias(first_keys, [hkey])
        A, At, X, T2, Xt, scr = newt
        for i in range(NT):
            kf = kfm[:, i * 128:(i + 1) * 128]
            qf = qfm[:, i * 128:(i + 1) * 128]
            kt = ktm[:, i * DS:(i + 1) * DS]
            pb, pbk = B.ps_bank()
            B.pe(lambda e, p=pb, kf=kf: e.matmul(p[:, 0:128], kf, kf,
                                                 start=True, stop=True),
                 [kfm_k], [pbk], f"{tag}:G{i}")
            B.dve(lambda e, p=pb: e.tensor_tensor(out=A, in0=p[:, 0:128],
                                                  in1=lrtril, op=ALU.mult),
                  [pbk, "consts", nk("A")], [nk("A")], f"{tag}:A{i}")
            B.dve(lambda e: e.tensor_tensor(out=A, in0=A, in1=identf, op=ALU.add),
                  [nk("A"), "consts"], [nk("A")], f"{tag}:A2{i}")
            B.dve(lambda e, p=pb: e.tensor_tensor(out=At, in0=p[:, 0:128],
                                                  in1=lrtriu, op=ALU.mult),
                  [pbk, "consts", nk("At")], [nk("At")], f"{tag}:At{i}")
            B.dve(lambda e: e.tensor_tensor(out=At, in0=At, in1=identf, op=ALU.add),
                  [nk("At"), "consts"], [nk("At")], f"{tag}:At2{i}")
            B.dve(lambda e: e.scalar_tensor_tensor(out=X, in0=A, scalar=-1.0,
                                                   in1=c2i, op0=ALU.mult, op1=ALU.add),
                  [nk("A"), "consts", nk("X")], [nk("X")], f"{tag}:X0{i}")
            for it in range(6):
                pb1, pbk1 = B.ps_bank()
                B.pe(lambda e, p=pb1: e.matmul(p[:, 0:128], At, X,
                                               start=True, stop=True),
                     [nk("At"), nk("X")], [pbk1], f"{tag}:nA{i}.{it}")
                B.dve(lambda e, p=pb1: e.scalar_tensor_tensor(
                    out=T2, in0=p[:, 0:128], scalar=-1.0, in1=c2i,
                    op0=ALU.mult, op1=ALU.add),
                      [pbk1, "consts", nk("T2")], [nk("T2")], f"{tag}:nT{i}.{it}")
                pb2, pbk2 = B.ps_bank()
                B.transpose(pb2[:, 0:128], pbk2, X, [nk("X")], bf16=False,
                            tag=f"{tag}:nXt{i}.{it}")
                B.act(lambda e, p=pb2: e.copy(out=Xt, in_=p[:, 0:128]),
                      [pbk2, nk("Xt")], [nk("Xt")], f"{tag}:nXte{i}.{it}")
                pb3, pbk3 = B.ps_bank()
                B.pe(lambda e, p=pb3: e.matmul(p[:, 0:128], Xt, T2,
                                               start=True, stop=True),
                     [nk("Xt"), nk("T2")], [pbk3], f"{tag}:nX2{i}.{it}")
                B.act(lambda e, p=pb3: e.copy(out=X, in_=p[:, 0:128]),
                      [pbk3, nk("X")], [nk("X")], f"{tag}:nXe{i}.{it}")
            pb4, pbk4 = B.ps_bank()
            B.transpose(pb4[:, 0:128], pbk4, X, [nk("X")], bf16=False,
                        tag=f"{tag}:mtv{i}")
            B.act(lambda e, p=pb4, i=i: e.copy(
                out=minvt[:, i * 128:(i + 1) * 128], in_=p[:, 0:128]),
                  [pbk4], [f"{tag}mv:{i}"], f"{tag}:mtve{i}")
            pb5, pbk5 = B.ps_bank()
            B.pe(lambda e, p=pb5, kf=kf, qf=qf: e.matmul(p[:, 0:128], kf, qf,
                                                         start=True, stop=True),
                 [kfm_k, qfm_k], [pbk5], f"{tag}:st{i}")
            B.dve(lambda e, p=pb5, i=i: e.tensor_tensor(
                out=smt[:, i * 128:(i + 1) * 128], in0=p[:, 0:128],
                in1=neglrtriu, op=ALU.mult),
                  [pbk5, "consts"], [f"{tag}sm:{i}"], f"{tag}:ste{i}")
            pb6, pbk6 = B.ps_bank()
            B.pe(lambda e, p=pb6, i=i, kt=kt: e.matmul(
                p[:, 0:DS], minvt[:, i * 128:(i + 1) * 128], kt,
                start=True, stop=True),
                 [f"{tag}mv:{i}", ktm_k], [pbk6], f"{tag}:U{i}")
            B.act(lambda e, p=pb6, i=i: e.copy(out=ubuf[:, i * DS:(i + 1) * DS],
                                               in_=p[:, 0:DS]),
                  [pbk6], [f"{tag}u:{i}"], f"{tag}:Ue{i}")
            pb7, pbk7 = B.ps_bank()
            B.pe(lambda e, p=pb7, kt=kt, i=i: e.matmul(
                p[0:DS, 0:DS], kt, ubuf[:, i * DS:(i + 1) * DS],
                start=True, stop=True),
                 [ktm_k, f"{tag}u:{i}"], [pbk7], f"{tag}:KtU{i}")
            B.dve(lambda e, p=pb7: e.scalar_tensor_tensor(
                out=scr[0:DS, 0:DS], in0=p[0:DS, 0:DS], scalar=-TTT_LR,
                in1=identf[0:DS, 0:DS], op0=ALU.mult, op1=ALU.add),
                  [pbk7, "consts", nk("scr")], [nk("scr")], f"{tag}:Mt{i}")
            pb8, pbk8 = B.ps_bank()
            B.transpose(pb8[0:DS, 0:DS], pbk8, scr[0:DS, 0:DS], [nk("scr")],
                        bf16=False, tag=f"{tag}:Mtr{i}")
            mslot = msb[:, i * 64:(i + 1) * 64]
            B.act(lambda e, p=pb8, mslot=mslot: e.copy(out=mslot[0:DS, :],
                                                       in_=p[0:DS, 0:DS]),
                  [pbk8], [f"{tag}M:{i}"], f"{tag}:Me{i}")

        B.dve(lambda e: e.memset(wst, 0.0), [f"{tag}wst"], [f"{tag}wst"],
              f"{tag}:wst0")
        B.dve(lambda e: e.tensor_copy(out=rt, in_=identf[0:64, 0:64]),
              ["consts", f"{tag}rt"], [f"{tag}rt"], f"{tag}:rt0")
        for i in range(NT):
            kf = kfm[:, i * 128:(i + 1) * 128]
            qf = qfm[:, i * 128:(i + 1) * 128]
            kt = ktm[:, i * DS:(i + 1) * DS]
            B.dve(lambda e: e.tensor_copy(out=w16, in_=wst),
                  [f"{tag}wst", f"{tag}w16"], [f"{tag}w16"], f"{tag}:w16c{i}")
            B.dve(lambda e: e.tensor_copy(out=rt16, in_=rt),
                  [f"{tag}rt", f"{tag}rt16"], [f"{tag}rt16"], f"{tag}:rt16c{i}")
            for ds in range(2):
                pb, pbk = B.ps_bank()
                B.pe(lambda e, p=pb, kf=kf, ds=ds: e.matmul(
                    p, kf, w16[:, ds * 512:(ds + 1) * 512], start=True, stop=True),
                     [kfm_k, f"{tag}w16"], [pbk], f"{tag}:B{i}.{ds}")
                B.dve(lambda e, p=pb, i=i, ds=ds: e.tensor_tensor(
                    out=b16[:, ds * 512:(ds + 1) * 512], in0=p,
                    in1=vtm[:, i * D + ds * 512:i * D + (ds + 1) * 512],
                    op=ALU.subtract),
                      [pbk, vtm_k, f"{tag}b16"], [f"{tag}b16"], f"{tag}:Be{i}.{ds}")
                pb2, pbk2 = B.ps_bank()
                B.pe(lambda e, p=pb2, i=i, ds=ds: e.matmul(
                    p, minvt[:, i * 128:(i + 1) * 128],
                    b16[:, ds * 512:(ds + 1) * 512], start=True, stop=True),
                     [f"{tag}mv:{i}", f"{tag}b16"], [pbk2], f"{tag}:E{i}.{ds}")
                B.act(lambda e, p=pb2, ds=ds: e.copy(
                    out=e16[:, ds * 512:(ds + 1) * 512], in_=p),
                      [pbk2, f"{tag}e16"], [f"{tag}e16"], f"{tag}:Ee{i}.{ds}")
            for ds in range(2):
                pb, pbk = B.ps_bank()
                def yfn(e, p=pb, qf=qf, i=i, ds=ds):
                    e.matmul(p, qf, w16[:, ds * 512:(ds + 1) * 512],
                             start=True, stop=False)
                    return e.matmul(p, smt[:, i * 128:(i + 1) * 128],
                                    e16[:, ds * 512:(ds + 1) * 512],
                                    start=False, stop=True)
                B.pe(yfn, [qfm_k, f"{tag}w16", f"{tag}sm:{i}", f"{tag}e16"],
                     [pbk], f"{tag}:y{i}.{ds}")
                xs = B.x_sb[:, i * D + ds * 512:i * D + (ds + 1) * 512]
                B.dve(lambda e, p=pb, xs=xs: e.tensor_tensor(out=xs, in0=xs,
                                                             in1=p, op=ALU.add),
                      [pbk, f"x:{i}"], [f"x:{i}"], f"{tag}:xy{i}.{ds}")
            for ds in range(2):
                pb, pbk = B.ps_bank()
                B.pe(lambda e, p=pb, kt=kt, ds=ds: e.matmul(
                    p[0:DS, :], kt, e16[:, ds * 512:(ds + 1) * 512],
                    start=True, stop=True),
                     [ktm_k, f"{tag}e16"], [pbk], f"{tag}:Wu{i}.{ds}")
                B.dve(lambda e, p=pb, ds=ds: e.scalar_tensor_tensor(
                    out=wst[:, ds * 512:(ds + 1) * 512], in0=p[0:DS, :],
                    scalar=-TTT_LR, in1=wst[:, ds * 512:(ds + 1) * 512],
                    op0=ALU.mult, op1=ALU.add),
                      [pbk, f"{tag}wst"], [f"{tag}wst"], f"{tag}:Wue{i}.{ds}")
            pb, pbk = B.ps_bank()
            B.pe(lambda e, p=pb, i=i: e.matmul(
                p[0:DS, 0:128], ubuf[:, i * DS:(i + 1) * DS],
                smt[:, i * 128:(i + 1) * 128], start=True, stop=True),
                 [f"{tag}u:{i}", f"{tag}sm:{i}"], [pbk], f"{tag}:qt{i}")
            B.dve(lambda e, p=pb, qf=qf: e.tensor_tensor(
                out=qt16, in0=p[0:DS, 0:128], in1=qf, op=ALU.add),
                  [pbk, qfm_k, f"{tag}qt16"], [f"{tag}qt16"], f"{tag}:qte{i}")
            pb, pbk = B.ps_bank()
            B.pe(lambda e, p=pb: e.matmul(p[0:DS, 0:128], rt16, qt16,
                                          start=True, stop=True),
                 [f"{tag}rt16", f"{tag}qt16"], [pbk], f"{tag}:H{i}")
            B.act(lambda e, p=pb, i=i: e.copy(
                out=h16[:, i * 128:(i + 1) * 128], in_=p[0:DS, 0:128]),
                  [pbk], [f"{tag}h16:{i}"], f"{tag}:He{i}")
            mslot = msb[:, i * 64:(i + 1) * 64]
            pb, pbk = B.ps_bank()
            B.pe(lambda e, p=pb, mslot=mslot: e.matmul(
                p[0:DS, 0:DS], mslot[0:DS, :], rt[:, :], start=True, stop=True),
                 [f"{tag}M:{i}", f"{tag}rt"], [pbk], f"{tag}:R{i}")
            B.act(lambda e, p=pb: e.copy(out=rt, in_=p[0:DS, 0:DS]),
                  [pbk, f"{tag}rt"], [f"{tag}rt"], f"{tag}:Re{i}")

        self.dbg(f"{tag}wst", wst, [64, 1024], [f"{tag}wst"])
        self.dbg(f"{tag}h16", h16, [64, 16 * 128], [f"{tag}h16:{i}" for i in range(NT)], BF16)
        u = self.uid()
        cc_in = B.internal(f"tcin{u}", [64, 1088], F32)
        cc_out = B.internal(f"tcout{u}", [256, 1088], F32)
        pb, pbk = B.ps_bank()
        B.transpose(pb[0:DS, 0:DS], pbk, rt[:, :], [f"{tag}rt"], bf16=False,
                    tag=f"{tag}:C")
        B.act(lambda e, p=pb: e.copy(out=csb, in_=p[0:DS, 0:DS]),
              [pbk, f"{tag}csb"], [f"{tag}csb"], f"{tag}:Ce")
        B.dma("SP", cc_in[:, 0:1024], wst, [f"{tag}wst"], [f"tcin{u}"], f"{tag}:ccw")
        B.dma("SP", cc_in[:, 1024:1088], csb, [f"{tag}csb"], [f"tcin{u}"],
              f"{tag}:ccc")
        B.collective(cc_in, cc_out, [f"tcin{u}"], [f"tcout{u}"], f"{tag}:ag")
        B.dve(lambda e: e.memset(wcs, 0.0), [f"{tag}wcs"], [f"{tag}wcs"],
              f"{tag}:wcs0")
        sm = B.sb["smalls"]
        for r in range(4):
            gs = gsl[:, (r % 2) * 1088:(r % 2) * 1088 + 1088]
            gk = f"{tag}gs{r % 2}"
            B.dma("SP", gs, cc_out[r * 64:(r + 1) * 64, :], [f"tcout{u}", gk],
                  [gk], f"{tag}:gin{r}")
            for ds in range(2):
                pb, pbk = B.ps_bank()
                B.pe(lambda e, p=pb, gs=gs, ds=ds: e.matmul(
                    p[0:DS, :], gs[:, 1024:1088], wcs[:, ds * 512:(ds + 1) * 512],
                    start=True, stop=True),
                     [gk, f"{tag}wcs"], [pbk], f"{tag}:cw{r}.{ds}")
                B.dve(lambda e, p=pb, gs=gs, ds=ds: e.tensor_tensor(
                    out=ctmp, in0=p[0:DS, :], in1=gs[:, ds * 512:(ds + 1) * 512],
                    op=ALU.add),
                      [pbk, gk, f"{tag}ctmp"], [f"{tag}ctmp"], f"{tag}:ca{r}.{ds}")
                B.dve(lambda e, ds=ds: e.tensor_tensor(
                    out=ctmp, in0=ctmp, in1=wcs[:, ds * 512:(ds + 1) * 512],
                    op=ALU.subtract),
                      [f"{tag}ctmp", f"{tag}wcs"], [f"{tag}ctmp"], f"{tag}:cs{r}.{ds}")
                B.act(lambda e, r=r: e.activation(out=ctmp, in_=ctmp, func=AF.Copy,
                                                  scale=sm[0:DS, 4 + r:5 + r]),
                      [f"{tag}ctmp", "smalls"], [f"{tag}ctmp"], f"{tag}:cm{r}.{ds}")
                B.dve(lambda e, ds=ds: e.tensor_tensor(
                    out=wcs[:, ds * 512:(ds + 1) * 512],
                    in0=wcs[:, ds * 512:(ds + 1) * 512], in1=ctmp, op=ALU.add),
                      [f"{tag}ctmp", f"{tag}wcs"], [f"{tag}wcs"], f"{tag}:cu{r}.{ds}")
        B.dve(lambda e: e.tensor_copy(out=wcs16, in_=wcs),
              [f"{tag}wcs", f"{tag}wcs16"], [f"{tag}wcs16"], f"{tag}:wc16")
        self.dbg(f"{tag}wcs", wcs, [64, 1024], [f"{tag}wcs"])
        for i in range(NT):
            for ds in range(2):
                pb, pbk = B.ps_bank()
                B.pe(lambda e, p=pb, i=i, ds=ds: e.matmul(
                    p, h16[:, i * 128:(i + 1) * 128],
                    wcs16[:, ds * 512:(ds + 1) * 512], start=True, stop=True),
                     [f"{tag}h16:{i}", f"{tag}wcs16"], [pbk], f"{tag}:dy{i}.{ds}")
                xs = B.x_sb[:, i * D + ds * 512:i * D + (ds + 1) * 512]
                B.dve(lambda e, p=pb, xs=xs: e.tensor_tensor(out=xs, in0=xs,
                                                             in1=p, op=ALU.add),
                      [pbk, f"x:{i}"], [f"x:{i}"], f"{tag}:xdy{i}.{ds}")


    def emit_moe(self, tag):
        B = self
        w = B.w
        ar = B.arena
        rec = B.rec
        dump = B.sb["dump"][:, :]
        NE, MD = 8, 4096

        self._moe_count = getattr(self, "_moe_count", 0) + 1
        hd_name = f"hD{self._moe_count % 2}"
        h_D = B.internal(hd_name, [MD, T], BF16)

        wg_dr = B.inp("m_wg", [D, 8], F32, lambda c: w["L1_ffn_w_gate"])
        bg_dr = B.inp("m_bg", [1, 8], F32, lambda c: w["L1_ffn_b_gate"].reshape(1, 8))
        w1_dr = B.inp("m_w1", [NE * D, MD], F32,
                      lambda c: w["L1_ffn_W1"].reshape(NE * D, MD))
        w2_dr = B.inp("m_w2", [NE * MD, D], F32,
                      lambda c: w["L1_ffn_W2"].reshape(NE * MD, D))
        b1c_dr = B.inp("m_b1c", [128, NE * 32], F32, lambda c: np.ascontiguousarray(
            np.concatenate([w["L1_ffn_b1"][e].reshape(32, 128).T for e in range(NE)], 1)))
        b2_dr = B.inp("m_b2", [NE, D], F32, lambda c: w["L1_ffn_b2"])

        zT = ar.alloc_bf16(NC * T, "zT")
        w2slot = ar.alloc_f32(16384, "w2slot")
        w1b = [ar.alloc_bf16(NC * 512, f"w1b{j}") for j in range(2)]
        hstage = [ar.alloc_bf16(512, f"hst{j}") for j in range(2)]
        hbuf = [ar.alloc_bf16(4 * 256, f"hbf{j}") for j in range(2)]
        b2b = ar.alloc_f32(D, "b2b")
        coef = ar.alloc_f32(NT * 8, "coef")
        wg = ar.alloc_f32(NC * 8, "wg")
        bg_b = ar.alloc_f32(8, "bg_b")
        gst = ar.alloc_f32(48, "gst")
        w2sb = w2slot.bitcast(BF16)
        zbuf = [w2slot[:, 0:1024], w2slot[:, 1024:2048]]
        zfm32 = [w2slot[:, 2048:3072], w2slot[:, 3072:4096]]
        zT_k = f"{tag}zT"
        w2_k = f"{tag}w2"
        rec.alias([w2_k], [f"{tag}zb0", f"{tag}zb1", f"{tag}zf0", f"{tag}zf1"])

        B.dma("SP", wg.rearrange("p (k c) -> p k c", k=NC),
              wg_dr.rearrange("(k p) c -> p k c", p=128), [], [f"{tag}wg"],
              f"{tag}:ldwg")
        self.bcast_row(bg_dr, "bg", bg_b, f"{tag}bg", f"{tag}gq", 8)
        b1c = ar.alloc_f32(NE * 32, "b1c")
        B.dma("SP", b1c, b1c_dr, [], [f"{tag}b1c"], f"{tag}:ldb1c")

        def zcb(i):
            return zbuf[i % 2], f"{tag}zb{i % 2}"

        gsums = gst[:, 0:8]
        grmax = gst[:, 8:9]
        gnrmax = gst[:, 9:10]
        gexp = gst[:, 16:24]
        gssum = gst[:, 24:25]
        gpmax = gst[:, 25:26]
        gmask = gst[:, 32:40]

        def post(i):
            zf = zfm32[i % 2]
            zfk = f"{tag}zf{i % 2}"
            zk = f"{tag}zb{i % 2}"
            for k in range(NC):
                pb, pbk = B.ps_bank()
                B.transpose(pb[:, 0:128], pbk,
                            zbuf[i % 2][:, k * 128:(k + 1) * 128], [zk],
                            bf16=False, tag=f"{tag}:tr{i}.{k}")
                B.act(lambda e, p=pb, zf=zf, k=k: e.copy(
                    out=zf[:, k * 128:(k + 1) * 128], in_=p[:, 0:128]),
                      [pbk, zfk], [zfk], f"{tag}:te{i}.{k}")
                B.dve(lambda e, zf=zf, k=k, i=i: e.tensor_copy(
                    out=zT[:, k * T + i * 128:k * T + (i + 1) * 128],
                    in_=zf[:, k * 128:(k + 1) * 128]),
                      [zfk], [zT_k], f"{tag}:zc{i}.{k}")
            pb, pbk = B.ps_bank()
            pairs = [(zf[:, k * 128:(k + 1) * 128], wg[:, k * 8:(k + 1) * 8])
                     for k in range(NC)]
            self.mm_group(pb[:, 0:8], pbk, pairs, [zfk, f"{tag}wg"],
                          f"{tag}:gmm{i}")
            gk = f"{tag}gst"
            B.dve(lambda e, p=pb: e.tensor_tensor(out=gsums, in0=p[:, 0:8],
                                                  in1=bg_b[:, 0:8], op=ALU.add),
                  [pbk, f"{tag}gq", gk], [gk], f"{tag}:gb{i}")
            B.dve(lambda e: e.reduce_max(out=grmax, in_=gsums, axis=AX),
                  [gk], [gk], f"{tag}:gm{i}")
            B.dve(lambda e: e.tensor_scalar_mul(out=gnrmax, in0=grmax, scalar1=-1.0),
                  [gk], [gk], f"{tag}:gn{i}")
            B.act(lambda e: e.activation(out=gexp, in_=gsums, func=AF.Exp,
                                         bias=gnrmax),
                  [gk], [gk], f"{tag}:ge{i}")
            B.dve(lambda e: e.reduce_sum(out=gssum, in_=gexp, axis=AX),
                  [gk], [gk], f"{tag}:gs{i}")
            B.dve(lambda e: e.reciprocal(out=gpmax, in_=gssum),
                  [gk], [gk], f"{tag}:gp{i}")
            B.dve(lambda e: e.tensor_scalar(out=gmask, in0=gsums, scalar1=grmax,
                                            scalar2=None, op0=ALU.is_ge),
                  [gk], [gk], f"{tag}:gq{i}")
            B.dve(lambda e, i=i: e.tensor_scalar(
                out=coef[:, i * 8:(i + 1) * 8], in0=gmask, scalar1=gpmax,
                scalar2=0.5, op0=ALU.mult, op1=ALU.mult),
                  [gk], [f"{tag}coef"], f"{tag}:gc{i}")
        self.emit_ln(zcb, f"{tag}ln", post_cb=post)
        self.dbg(f"{tag}coef", coef, [128, NT * 8], [f"{tag}coef"])
        self.dbg(f"{tag}zT", zT, [128, NC * T], [zT_k], BF16)

        zT_keys = [zT_k]
        for e_ix in range(NE):
            ek = f"{tag}e{e_ix}"
            for qw in range(4):
                B.dma("POOL", w2sb[:, qw * 8192:(qw + 1) * 8192].rearrange(
                    "p (k c) -> p k c", k=8),
                      w2_dr[e_ix * MD + qw * 1024:e_ix * MD + (qw + 1) * 1024,
                            :].rearrange("(k p) c -> p k c", p=128),
                      [], [w2_k], f"{ek}:ldw2.{qw}")
            B.dma("SP", dump[0:1, 0:D], b2_dr[e_ix:e_ix + 1, :], ["dump"],
                  ["dump"], f"{ek}:b2row")
            ones = B.sb["onesrow"][:, :]
            for j in range(2):
                pb, pbk = B.ps_bank()
                B.pe(lambda e, p=pb, j=j: e.matmul(
                    p, ones, dump[0:1, j * 512:(j + 1) * 512],
                    start=True, stop=True),
                     ["dump", "onesrow"], [pbk], f"{ek}:b2m{j}")
                B.act(lambda e, p=pb, j=j: e.copy(
                    out=b2b[:, j * 512:(j + 1) * 512], in_=p),
                      [pbk, f"{tag}b2b"], [f"{tag}b2b"], f"{ek}:b2e{j}")
            if DEBUG and e_ix == NE - 1:
                self.dbg(f"{tag}w2t", w2sb[:, 28 * 1024:32 * 1024], [128, 4096],
                         [w2_k], BF16)
            for q8 in range(8):
                wb = w1b[q8 % 2]
                wbk = f"{ek}w1_{q8 % 2}"
                B.dma("POOL", wb.rearrange("p (k c) -> p k c", k=NC),
                      w1_dr[e_ix * D:(e_ix + 1) * D,
                            q8 * 512:(q8 + 1) * 512].rearrange(
                          "(k p) c -> p k c", p=128),
                      [], [wbk], f"{ek}:ldw1.{q8}")
                for mt in range(4):
                    mg = q8 * 4 + mt
                    bcol = b1c[:, e_ix * 32 + mg:e_ix * 32 + mg + 1]
                    for tsl in range(4):
                        pb, pbk = B.ps_bank()
                        pairs = [(wb[:, k * 512 + mt * 128:k * 512 + (mt + 1) * 128],
                                  zT[:, k * T + tsl * 512:k * T + (tsl + 1) * 512])
                                 for k in range(NC)]
                        self.mm_group(pb, pbk, pairs, [wbk] + zT_keys,
                                      f"{ek}:hmm{mg}.{tsl}")
                        hs = hstage[tsl % 2]
                        hsk = f"{ek}hs{tsl % 2}"
                        B.act(lambda e, p=pb, hs=hs, bcol=bcol: e.activation(
                            out=hs, in_=p, func=AF.Relu, bias=bcol),
                              [pbk, f"{tag}b1c", hsk], [hsk], f"{ek}:hrelu{mg}.{tsl}")
                        B.dma("ACT", h_D[mg * 128:(mg + 1) * 128,
                                         tsl * 512:(tsl + 1) * 512],
                              hs,
                              [hsk], [f"{tag}hD:{mg}"], f"{ek}:hw{mg}.{tsl}")
            for tg in range(8):
                banks = [self.ps_bank() for _ in range(4)]
                for hq in range(8):
                    hb = hbuf[hq % 2]
                    hbk = f"{ek}hb{hq % 2}"
                    B.dma("SP", hb.rearrange("p (k t) -> p k t", k=4),
                          h_D[hq * 512:(hq + 1) * 512,
                              tg * 256:(tg + 1) * 256].rearrange(
                              "(k p) t -> p k t", p=128),
                          [f"{tag}hD:{hq * 4}", f"{tag}hD:{hq * 4 + 1}",
                           f"{tag}hD:{hq * 4 + 2}", f"{tag}hD:{hq * 4 + 3}", hbk],
                          [hbk], f"{ek}:hr{tg}.{hq}")
                    def bmm(e, hb=hb, hq=hq, tg=tg, banks=banks):
                        last = None
                        for mk in range(4):
                            for tt in range(2):
                                for dh in range(2):
                                    p, _ = banks[tt * 2 + dh]
                                    last = e.matmul(
                                        p,
                                        hb[:, mk * 256 + tt * 128:
                                           mk * 256 + (tt + 1) * 128],
                                        w2sb[:, (hq * 4 + mk) * 1024 + dh * 512:
                                             (hq * 4 + mk) * 1024 + (dh + 1) * 512],
                                        start=(hq == 0 and mk == 0),
                                        stop=(hq == 7 and mk == 3))
                        return last
                    B.pe(bmm, [hbk, w2_k] + [bk for _, bk in banks],
                         [bk for _, bk in banks], f"{ek}:bmm{tg}.{hq}")
                if DEBUG and e_ix == NE - 1:
                    self.dbg(f"{tag}hD{tg}", h_D[0:128, tg * 256:(tg + 1) * 256],
                             [128, 256], [f"{tag}hD:0"], BF16)
                    self.dbg(f"{tag}hE{tg}", h_D[29 * 128:30 * 128,
                                                 tg * 256:(tg + 1) * 256],
                             [128, 256], [f"{tag}hD:29"], BF16)
                for tt in range(2):
                    for dh in range(2):
                        p, pk = banks[tt * 2 + dh]
                        ti = tg * 2 + tt
                        B.dve(lambda e, p=p, dh=dh: e.tensor_tensor(
                            out=dump[:, 0:512], in0=p,
                            in1=b2b[:, dh * 512:(dh + 1) * 512], op=ALU.add),
                              [pk, f"{tag}b2b", "dump"], ["dump"],
                              f"{ek}:yb{tg}.{tt}.{dh}")
                        if DEBUG and e_ix == NE - 1 and tg == 0 and tt == 0:
                            self.dbg(f"{tag}y{dh}", dump[:, 0:512], [128, 512],
                                     ["dump"])
                        xs = B.x_sb[:, ti * D + dh * 512:ti * D + (dh + 1) * 512]
                        B.dve(lambda e, xs=xs, ti=ti, ei=e_ix: e.scalar_tensor_tensor(
                            out=xs, in0=dump[:, 0:512],
                            scalar=coef[:, ti * 8 + ei:ti * 8 + ei + 1],
                            in1=xs, op0=ALU.mult, op1=ALU.add),
                              ["dump", f"{tag}coef", f"x:{ti}"], [f"x:{ti}"],
                              f"{ek}:xa{tg}.{tt}.{dh}")
            if DEBUG:
                self.dbg(f"{tag}xe{e_ix}", B.x_sb[:, 0:1024], [128, 1024],
                         ["x:0"])


    def emit_l1attn(self, tag):
        B = self
        w = B.w
        ar = B.arena
        rec = B.rec
        dump = B.sb["dump"][:, :]
        DD = 2048
        NDK = 16
        SCALE = 1.0 / float(np.sqrt(DD))

        u = self.uid()
        qt_D = B.internal(f"qtD{u}", [DD, T], BF16)
        kt_D = B.internal(f"ktD{u}", [DD, T], BF16)
        v_D = B.internal(f"vD{u}", [T, DD], BF16)
        g_D = B.internal(f"gD{u}", [DD, T], BF16)
        sc_in = B.internal(f"scin{u}", [DD, DD], BF16)
        sc_out = B.internal(f"scout{u}", [4 * DD, DD], BF16)
        scores_D = B.internal(f"scD{u}", [NT * 128, T], BF16)
        attn1_D = B.internal(f"a1D{u}", [DD, T], BF16)
        attn_D = B.internal(f"atD{u}", [DD, T], BF16)

        wq_dr = B.inp("a_wq", [D, DD], F32, lambda c: w["L1_attn_Wq"])
        wk_dr = B.inp("a_wk", [D, DD], F32, lambda c: w["L1_attn_Wk"])
        wv_dr = B.inp("a_wv", [D, DD], F32, lambda c: w["L1_attn_Wv"])
        wg_dr = B.inp("a_wg", [D, DD], F32, lambda c: w["L1_attn_Wg"])
        wo_dr = B.inp("a_wo", [DD, D], F32, lambda c: w["L1_attn_Wo"])
        bcols_dr = B.inp("a_bcols", [128, 48], F32, lambda c: np.ascontiguousarray(
            np.concatenate([w["L1_attn_bq"].reshape(16, 128).T,
                            w["L1_attn_bk"].reshape(16, 128).T,
                            w["L1_attn_bg"].reshape(16, 128).T], 1)))
        bv_dr = B.inp("a_bv", [1, DD], F32, lambda c: w["L1_attn_bv"].reshape(1, DD))
        bo_dr = B.inp("a_bo", [1, D], F32, lambda c: w["L1_attn_bo"].reshape(1, D))

        slotA = ar.alloc_f32(16384, "slotA")
        slotB = ar.alloc_f32(16384, "slotB")
        aux = ar.alloc_f32(1464, "aux")
        bcols = aux[:, 0:48]
        ktt = [aux[:, 48:112].bitcast(BF16), aux[:, 112:176].bitcast(BF16)]
        pstage = [aux[:, 176:432].bitcast(BF16), aux[:, 432:688].bitcast(BF16)]
        sstb = aux[:, 176:432].bitcast(BF16)
        scst = aux[:, 688:944].bitcast(BF16)
        dbufs = aux[:, 944:1464]
        gbuf = [aux[:, 432:688].bitcast(BF16), aux[:, 688:944].bitcast(BF16)]
        atst = [aux[:, 944:1200].bitcast(BF16), aux[:, 1200:1456].bitcast(BF16)]
        rec.alias([f"{tag}at0", f"{tag}at1"], [f"{tag}dbufs"])
        rec.alias([f"{tag}gb0"], [f"{tag}pst1"])
        rec.alias([f"{tag}gb1"], [f"{tag}scst"])

        B.dma("SP", bcols, bcols_dr, ["aux"], [f"{tag}bc"], f"{tag}:bcld")

        hfm = slotA[:, 0:8192].bitcast(BF16)
        hkey = f"{tag}hfm"
        wsl = slotA[:, 8192:16384].bitcast(BF16)
        zt = [slotB[:, 0:512].bitcast(BF16), slotB[:, 512:1024].bitcast(BF16)]
        bv_b = slotB[:, 1024:3072]
        vsb = slotA.bitcast(BF16)
        ktcol = [slotA[:, 0:1024].bitcast(BF16),
                 slotA[:, 1024:2048].bitcast(BF16)]
        qtsb = slotB.bitcast(BF16)

        def zcb(i):
            return zt[i % 2], f"{tag}zt{i % 2}"

        def post(i):
            for k in range(NC):
                pb, pbk = B.psb_bank()
                B.transpose(pb[:, 0:128], pbk, zt[i % 2][:, k * 128:(k + 1) * 128],
                            [f"{tag}zt{i % 2}"], tag=f"{tag}:tr{i}.{k}")
                dst = hfm[:, k * T + i * 128:k * T + (i + 1) * 128]
                B.act(lambda e, p=pb, dst=dst: e.copy(out=dst, in_=p[:, 0:128]),
                      [pbk], [hkey], f"{tag}:te{i}.{k}")
        self.emit_ln(zcb, f"{tag}ln", post_cb=post)

        rec.alias([f"{tag}bvb"], [f"{tag}zt0", f"{tag}zt1"])
        self.bcast_row(bv_dr[:, 0:1024], "bv", bv_b[:, 0:1024], f"{tag}bvb",
                       f"{tag}bv1", 1024)
        self.bcast_row(bv_dr[:, 1024:2048], "bv2", bv_b[:, 1024:2048], f"{tag}bvb",
                       f"{tag}bv2", 1024)

        projs = [("q", wq_dr, qt_D, 0, False), ("k", wk_dr, kt_D, 16, False),
                 ("g", wg_dr, g_D, 32, True), ("v", wv_dr, v_D, None, False)]
        prev_wk = None
        for name, wdr, outD, bco, sigm in projs:
            wk_ = f"{tag}w{name}"
            if prev_wk:
                rec.alias([wk_], [prev_wk])
            prev_wk = wk_
            for half in range(2):
                B.dma("POOL", wsl[:, half * NC * 1024:(half + 1) * NC * 1024]
                      .rearrange("p (k c) -> p k c", k=NC),
                      wdr[:, half * 1024:(half + 1) * 1024]
                      .rearrange("(k p) c -> p k c", p=128),
                      [], [wk_], f"{tag}:ld{name}.{half}")
            if name != "v":
                for dt in range(NDK):
                    half = dt // 8
                    for ts in range(4):
                        pb, pbk = B.ps_bank()
                        pairs = [(wsl[:, half * NC * 1024 + k * 1024
                                      + (dt % 8) * 128:half * NC * 1024 + k * 1024
                                      + (dt % 8 + 1) * 128],
                                  hfm[:, k * T + ts * 512:k * T + (ts + 1) * 512])
                                 for k in range(NC)]
                        self.mm_group(pb, pbk, pairs, [wk_, hkey],
                                      f"{tag}:{name}mm{dt}.{ts}")
                        pg = pstage[ts % 2]
                        psk = f"{tag}pst{ts % 2}"
                        bcol = bcols[:, bco + dt:bco + dt + 1]
                        if sigm:
                            B.act(lambda e, p=pb, bcol=bcol, pg=pg: e.activation(
                                out=pg, in_=p, func=AF.Sigmoid, bias=bcol),
                                  [pbk, f"{tag}bc", psk], [psk],
                                  f"{tag}:{name}s{dt}.{ts}")
                        else:
                            B.dve(lambda e, p=pb, bcol=bcol, pg=pg: e.tensor_scalar(
                                out=pg, in0=p, scalar1=bcol, scalar2=None,
                                op0=ALU.add),
                                  [pbk, f"{tag}bc", psk], [psk],
                                  f"{tag}:{name}s{dt}.{ts}")
                        B.dma("ACT", outD[dt * 128:(dt + 1) * 128,
                                          ts * 512:(ts + 1) * 512], pg,
                              [psk], [f"{tag}{name}D:{dt}"], f"{tag}:{name}w{dt}.{ts}")
            else:
                for i in range(NT):
                    for ds in range(4):
                        half = ds // 2
                        pb, pbk = B.ps_bank()
                        pairs = [(hfm[:, k * T + i * 128:k * T + (i + 1) * 128],
                                  wsl[:, half * NC * 1024 + k * 1024
                                      + (ds % 2) * 512:half * NC * 1024 + k * 1024
                                      + (ds % 2) * 512 + 512])
                                 for k in range(NC)]
                        self.mm_group(pb, pbk, pairs, [wk_, hkey],
                                      f"{tag}:vmm{i}.{ds}")
                        pg = pstage[ds % 2]
                        psk = f"{tag}pst{ds % 2}"
                        B.dve(lambda e, p=pb, ds=ds, pg=pg: e.tensor_tensor(
                            out=pg, in0=p, in1=bv_b[:, ds * 512:(ds + 1) * 512],
                            op=ALU.add),
                              [pbk, f"{tag}bvb", psk], [psk], f"{tag}:vs{i}.{ds}")
                        B.dma("ACT", v_D[i * 128:(i + 1) * 128,
                                         ds * 512:(ds + 1) * 512], pg,
                              [psk], [f"{tag}vD:{i}"], f"{tag}:vw{i}.{ds}")

        rec.alias([f"{tag}vsb"], [hkey, f"{tag}wv"])
        B.dma("SP", vsb.rearrange("p (k c) -> p k c", k=NT),
              v_D.rearrange("(k p) c -> p k c", p=128),
              [f"{tag}vD:{i}" for i in range(NT)], [f"{tag}vsb"], f"{tag}:vld")
        for dk in range(NDK):
            banks = [B.ps_bank() for _ in range(4)]
            for j in range(NT):
                kb = ktt[j % 2]
                kbk = f"{tag}ktt{j % 2}"
                B.dma_t("SP", kb, kt_D[dk * 128:(dk + 1) * 128,
                                       j * 128:(j + 1) * 128],
                        [f"{tag}kD:{dk}", kbk], [kbk], f"{tag}:ktt{dk}.{j}")
                def scmm(e, kb=kb, j=j, banks=banks):
                    last = None
                    for es in range(4):
                        p, _ = banks[es]
                        last = e.matmul(p, kb,
                                        vsb[:, j * DD + es * 512:j * DD + (es + 1) * 512],
                                        start=(j == 0), stop=(j == NT - 1))
                    return last
                B.pe(scmm, [kbk, f"{tag}vsb"] + [bk for _, bk in banks],
                     [bk for _, bk in banks], f"{tag}:scmm{dk}.{j}")
            for es in range(4):
                p, pk = banks[es]
                B.act(lambda e, p=p: e.copy(out=scst, in_=p),
                      [pk, f"{tag}scst"], [f"{tag}scst"], f"{tag}:sce{dk}.{es}")
                B.dma("ACT", sc_in[dk * 128:(dk + 1) * 128, es * 512:(es + 1) * 512],
                      scst, [f"{tag}scst"], [f"scin{u}"], f"{tag}:scw{dk}.{es}")
        def ccfn(eng):
            return eng.collective_compute(
                "AllGather", ALU.bypass, ins=[sc_in], outs=[sc_out],
                replica_groups=[[0, 1, 2, 3], [4, 5, 6, 7]])
        B.rec.op("POOL", "cc", ccfn, [f"scin{u}"], [f"scout{u}"], f"{tag}:ag")

        rec.alias([f"{tag}QT"], [f"{tag}zt0", f"{tag}zt1", f"{tag}bvb"])
        rec.alias([f"{tag}kc0", f"{tag}kc1"], [f"{tag}vsb"])
        rec.alias([f"{tag}sst"], [f"{tag}pst0"])
        B.dma("SP", qtsb.rearrange("p (k c) -> p k c", k=NDK),
              qt_D.rearrange("(k p) c -> p k c", p=128),
              [f"{tag}qD:{d}" for d in range(NDK)], [f"{tag}QT"], f"{tag}:qtld")
        consts = B.sb["consts"]
        causal = consts[:, 640:768]
        dscr = dbufs[:, 0:128]
        dexp = dbufs[:, 128:256]
        db16 = dbufs[:, 256:320].bitcast(BF16)
        dstat = dbufs[:, 512:520]
        for ib in range(4):
            for j in range(ib * 4 + 4):
                kc = ktcol[j % 2]
                kck = f"{tag}kc{j % 2}"
                B.dma("SP", kc.rearrange("p (k c) -> p k c", k=NDK),
                      kt_D[:, j * 128:(j + 1) * 128].rearrange("(k p) c -> p k c",
                                                               p=128),
                      [f"{tag}kD:{d}" for d in range(NDK)] + [kck], [kck],
                      f"{tag}:kcl{ib}.{j}")
                pb, pbk = B.ps_bank()
                pairs = [(kc[:, dk * 128:(dk + 1) * 128],
                          qtsb[:, dk * T + ib * 512:dk * T + (ib + 1) * 512])
                         for dk in range(NDK)]
                self.mm_group(pb, pbk, pairs, [kck, f"{tag}QT"],
                              f"{tag}:smm{ib}.{j}")
                B.act(lambda e, p=pb: e.copy(out=sstb, in_=p),
                      [pbk, f"{tag}sst"], [f"{tag}sst"], f"{tag}:sev{ib}.{j}")
                B.dma("ACT", scores_D[j * 128:(j + 1) * 128,
                                      ib * 512:(ib + 1) * 512], sstb,
                      [f"{tag}sst"], [f"{tag}sc:{j}:{ib}"], f"{tag}:ssw{ib}.{j}")
        for i in range(NT):
            kc = ktcol[i % 2]
            kck = f"{tag}kc{i % 2}"
            B.dma("SP", kc.rearrange("p (k c) -> p k c", k=NDK),
                  kt_D[:, i * 128:(i + 1) * 128].rearrange("(k p) c -> p k c",
                                                           p=128),
                  [f"{tag}kD:{d}" for d in range(NDK)] + [kck], [kck],
                  f"{tag}:dkl{i}")
            pb, pbk = B.ps_bank()
            pairs = [(qtsb[:, dk * T + i * 128:dk * T + (i + 1) * 128],
                      kc[:, dk * 128:(dk + 1) * 128]) for dk in range(NDK)]
            self.mm_group(pb[:, 0:128], pbk, pairs, [kck, f"{tag}QT"],
                          f"{tag}:dmm{i}")
            dk_ = f"{tag}dbufs"
            B.dve(lambda e, p=pb: e.scalar_tensor_tensor(
                out=dscr, in0=p[:, 0:128], scalar=SCALE, in1=causal,
                op0=ALU.mult, op1=ALU.add),
                  [pbk, "consts", dk_], [dk_], f"{tag}:dsc{i}")
            B.dve(lambda e: e.reduce_max(out=dstat[:, 0:1], in_=dscr, axis=AX),
                  [dk_], [dk_], f"{tag}:dmx{i}")
            B.dve(lambda e: e.tensor_scalar_mul(out=dstat[:, 1:2], in0=dstat[:, 0:1],
                                                scalar1=-1.0),
                  [dk_], [dk_], f"{tag}:dnm{i}")
            B.act(lambda e: e.activation(out=dexp, in_=dscr, func=AF.Exp,
                                         bias=dstat[:, 1:2]),
                  [dk_], [dk_], f"{tag}:dex{i}")
            B.dve(lambda e: e.reduce_sum(out=dstat[:, 2:3], in_=dexp, axis=AX),
                  [dk_], [dk_], f"{tag}:dsm{i}")
            B.dve(lambda e: e.reciprocal(out=dstat[:, 3:4], in_=dstat[:, 2:3]),
                  [dk_], [dk_], f"{tag}:drp{i}")
            B.act(lambda e: e.activation(out=db16, in_=dexp, func=AF.Copy,
                                         scale=dstat[:, 3:4]),
                  [dk_], [dk_], f"{tag}:dnr{i}")
            pb2, pbk2 = B.psb_bank()
            B.transpose(pb2[:, 0:128], pbk2, db16, [dk_], tag=f"{tag}:dtr{i}")
            B.act(lambda e, p=pb2: e.copy(out=sstb[:, 0:128], in_=p[:, 0:128]),
                  [pbk2, f"{tag}sst"], [f"{tag}sst"], f"{tag}:dte{i}")
            B.dma("ACT", scores_D[i * 128:(i + 1) * 128, i * 128:(i + 1) * 128],
                  sstb[:, 0:128], [f"{tag}sst"], [f"{tag}sc:{i}:{i // 4}"],
                  f"{tag}:dsw{i}")

        if DEBUG:
            self.dbg(f"{tag}scall", scores_D, [NT * 128, T],
                     [f"{tag}sc:{j}:{ib}" for j in range(NT) for ib in range(4)],
                     BF16)
            self.dbg(f"{tag}vall", v_D, [T, DD],
                     [f"{tag}vD:{i}" for i in range(NT)], BF16)
            for ibx, jx in [(1, 0), (2, 0), (2, 5), (3, 0)]:
                self.dbg(f"{tag}slab{ibx}_{jx}",
                         scores_D[jx * 128:(jx + 1) * 128,
                                  ibx * 512:(ibx + 1) * 512],
                         [128, 512], [f"{tag}sc:{jx}:{ibx}"], BF16)
            for di in [0, 7, 12]:
                self.dbg(f"{tag}diag{di}",
                         scores_D[di * 128:(di + 1) * 128, di * 128:(di + 1) * 128],
                         [128, 128], [f"{tag}sc:{di}:{di // 4}"], BF16)
        rec.alias([f"{tag}vsb2"], [f"{tag}kc0", f"{tag}kc1"])
        sstrip = slotB[:, 0:4096].bitcast(BF16)
        rec.alias([f"{tag}sstrip"], [f"{tag}QT"])
        B.dma("SP", vsb.rearrange("p (k c) -> p k c", k=NT),
              v_D.rearrange("(k p) c -> p k c", p=128),
              [f"{tag}vD:{i}" for i in range(NT)] + [f"{tag}vsb2"], [f"{tag}vsb2"],
              f"{tag}:vld2")
        allsc = [f"{tag}sc:{j}:{ibx}" for j in range(NT) for ibx in range(4)]
        for ib in range(4):
            jmax = ib * 4 + 4
            rec.barrier()
            B.dma("SP", sstrip[:, 0:jmax * 512].rearrange("p (k c) -> p k c", k=jmax),
                  scores_D[0:jmax * 128, ib * 512:(ib + 1) * 512]
                  .rearrange("(k p) c -> p k c", p=128),
                  allsc + [f"{tag}sstrip"], [f"{tag}sstrip"], f"{tag}:sld{ib}")
            if DEBUG and ib in (1, 2):
                self.dbg(f"{tag}strip{ib}", sstrip[:, 0:jmax * 512],
                         [128, jmax * 512], [f"{tag}sstrip"], BF16)
                self.dbg(f"{tag}vsb{ib}", vsb[:, 0:8192], [128, 8192],
                         [f"{tag}vsb2"], BF16)
            for dt in range(NDK):
                pb, pbk = B.ps_bank()
                def apfn(e, ib=ib, dt=dt, jmax=jmax, p=pb):
                    last = None
                    for j in range(jmax):
                        r = j - ib * 4
                        lhs = vsb[:, j * DD + dt * 128:j * DD + (dt + 1) * 128]
                        if r < 0:
                            last = e.matmul(p, lhs, sstrip[:, j * 512:(j + 1) * 512],
                                            start=(j == 0), stop=False)
                        else:
                            c0 = r * 128
                            last = e.matmul(p[:, c0:512], lhs,
                                            sstrip[:, j * 512 + c0:(j + 1) * 512],
                                            start=(j == 0), stop=(j == jmax - 1))
                    return last
                B.pe(apfn, [f"{tag}sstrip", f"{tag}vsb2"], [pbk],
                     f"{tag}:ap{ib}.{dt}")
                rec.barrier()
                at = atst[dt % 2]
                atk = f"{tag}at{dt % 2}"
                B.act(lambda e, p=pb, at=at: e.copy(out=at, in_=p),
                      [pbk, atk], [atk], f"{tag}:ae{ib}.{dt}")
                B.dma("ACT", attn1_D[dt * 128:(dt + 1) * 128,
                                     ib * 512:(ib + 1) * 512], at,
                      [atk], [f"{tag}a1:{dt}"], f"{tag}:a1w{ib}.{dt}")

        if DEBUG:
            for dtx in [0, 7]:
                self.dbg(f"{tag}a1d{dtx}", attn1_D[dtx * 128:(dtx + 1) * 128, :],
                         [128, T], [f"{tag}a1:{dtx}"], BF16)
        sm = B.sb["smalls"]
        pstrip = [slotA[:, 12288:13312].bitcast(BF16),
                  slotA[:, 13312:14336].bitcast(BF16)]
        rec.alias([f"{tag}ps0", f"{tag}ps1"], [f"{tag}vsb2"])
        rec.alias([f"{tag}kc0", f"{tag}kc1"], [f"{tag}vsb2"])
        rec.alias([f"{tag}QT2"], [f"{tag}sstrip"])
        B.dma("SP", qtsb.rearrange("p (k c) -> p k c", k=NDK),
              qt_D.rearrange("(k p) c -> p k c", p=128),
              [f"{tag}qD:{d}" for d in range(NDK)] + [f"{tag}QT2"], [f"{tag}QT2"],
              f"{tag}:qtld2")
        for dt in range(NDK):
            pst_ = pstrip[dt % 2]
            psk_ = f"{tag}ps{dt % 2}"
            for r in range(4):
                gsl = ktcol[r % 2]
                gk = f"{tag}kc{r % 2}"
                B.dma("SP", gsl.rearrange("p (k c) -> p k c", k=NDK),
                      sc_out[r * DD:(r + 1) * DD, dt * 128:(dt + 1) * 128]
                      .rearrange("(k p) c -> p k c", p=128),
                      [f"scout{u}", gk], [gk], f"{tag}:pg{dt}.{r}")
                if r == 0:
                    B.act(lambda e, gsl=gsl, pst_=pst_: e.activation(
                        out=pst_, in_=gsl, func=AF.Copy, scale=sm[:, 4:5]),
                          [gk, "smalls", psk_], [psk_], f"{tag}:p0.{dt}")
                else:
                    B.dve(lambda e, gsl=gsl, r=r, pst_=pst_: e.scalar_tensor_tensor(
                        out=pst_, in0=gsl, scalar=sm[:, 4 + r:5 + r], in1=pst_,
                        op0=ALU.mult, op1=ALU.add),
                          [gk, "smalls", psk_], [psk_], f"{tag}:p{r}.{dt}")
            for ib in range(4):
                pb, pbk = B.ps_bank()
                def dfn(e, dt=dt, ib=ib, p=pb, pst_=pst_):
                    last = None
                    for dq in range(NDK):
                        last = e.matmul(p, pst_[:, dq * 128:(dq + 1) * 128],
                                        qtsb[:, dq * T + ib * 512:
                                             dq * T + (ib + 1) * 512],
                                        start=(dq == 0), stop=(dq == NDK - 1))
                    return last
                B.pe(dfn, [psk_, f"{tag}QT2"], [pbk], f"{tag}:dx{ib}.{dt}")
                a1 = atst[0]
                a1k = f"{tag}at0"
                B.dma("SP", a1, attn1_D[dt * 128:(dt + 1) * 128,
                                        ib * 512:(ib + 1) * 512],
                      [f"{tag}a1:{dt}", a1k], [a1k], f"{tag}:a1l{ib}.{dt}")
                gb = gbuf[0]
                gbk = f"{tag}gb0"
                B.dma("SP", gb, g_D[dt * 128:(dt + 1) * 128,
                                    ib * 512:(ib + 1) * 512],
                      [f"{tag}gD:{dt}", gbk], [gbk], f"{tag}:gld{ib}.{dt}")
                B.dve(lambda e, p=pb, a1=a1: e.tensor_tensor(
                    out=dump[:, 0:512], in0=p, in1=a1, op=ALU.add),
                      [pbk, a1k, "dump"], ["dump"], f"{tag}:dadd{ib}.{dt}")
                at2 = atst[1]
                at2k = f"{tag}at1"
                B.dve(lambda e, gb=gb, at2=at2: e.tensor_tensor(
                    out=at2, in0=dump[:, 0:512], in1=gb, op=ALU.mult),
                      ["dump", gbk, at2k], [at2k], f"{tag}:dmul{ib}.{dt}")
                B.dma("ACT", attn_D[dt * 128:(dt + 1) * 128,
                                    ib * 512:(ib + 1) * 512], at2,
                      [at2k], [f"{tag}atD:{dt}"], f"{tag}:aw{ib}.{dt}")

        wo = slotA[:, 0:8192].bitcast(BF16)
        acol = [slotA[:, 8192:9216].bitcast(BF16),
                slotA[:, 9216:10240].bitcast(BF16)]
        bo_b = slotB[:, 0:1024]
        rec.alias([f"{tag}wo"], [f"{tag}vsb2", f"{tag}kc0", f"{tag}kc1"])
        rec.alias([f"{tag}ac0", f"{tag}ac1"], [f"{tag}vsb2", f"{tag}ps0", f"{tag}ps1"])
        rec.alias([f"{tag}bo"], [f"{tag}QT2"])
        self.bcast_row(bo_dr, "bo", bo_b, f"{tag}bo", f"{tag}boq", D)
        B.dma("POOL", wo.rearrange("p (k c) -> p k c", k=NDK),
              wo_dr.rearrange("(k p) c -> p k c", p=128), [], [f"{tag}wo"],
              f"{tag}:ldwo")
        for i in range(NT):
            ac = acol[i % 2]
            ack = f"{tag}ac{i % 2}"
            B.dma("SP", ac.rearrange("p (k c) -> p k c", k=NDK),
                  attn_D[:, i * 128:(i + 1) * 128].rearrange("(k p) c -> p k c",
                                                             p=128),
                  [f"{tag}atD:{d}" for d in range(NDK)] + [ack], [ack],
                  f"{tag}:ald{i}")
            for ds in range(2):
                pb, pbk = B.ps_bank()
                pairs = [(ac[:, dk * 128:(dk + 1) * 128],
                          wo[:, dk * D + ds * 512:dk * D + (ds + 1) * 512])
                         for dk in range(NDK)]
                self.mm_group(pb, pbk, pairs, [ack, f"{tag}wo"],
                              f"{tag}:omm{i}.{ds}")
                B.dve(lambda e, p=pb, ds=ds: e.tensor_tensor(
                    out=dump[:, 0:512], in0=p, in1=bo_b[:, ds * 512:(ds + 1) * 512],
                    op=ALU.add),
                      [pbk, f"{tag}bo", "dump"], ["dump"], f"{tag}:ob{i}.{ds}")
                xs = B.x_sb[:, i * D + ds * 512:i * D + (ds + 1) * 512]
                B.dve(lambda e, xs=xs: e.tensor_tensor(out=xs, in0=xs,
                                                       in1=dump[:, 0:512],
                                                       op=ALU.add),
                      ["dump", f"x:{i}"], [f"x:{i}"], f"{tag}:xo{i}.{ds}")


SUBLAYERS_ALL = ["l0ffn", "ttt", "l0ffn", "moe", "l1attn", "moe"]
LAUNCHES = [["l0ffn"], ["ttt"], ["l0ffn"], ["moe"], ["l1attn"], ["moe"]]


def _shard_x(x):
    xf = np.ascontiguousarray(np.asarray(x).reshape(NCORES * T, D))
    return [np.ascontiguousarray(xf[c * T:(c + 1) * T]) for c in range(NCORES)]


_last_results = None


def kernel(**inputs):
    global _last_results
    cur = _shard_x(inputs["L0_attn_x"])
    for li, subs in enumerate(LAUNCHES):
        b = Builder(f"launch{li}", subs)
        nc = b.build(inputs)
        in_maps = []
        for c in range(NCORES):
            m = {}
            for name, fn in b.input_specs.items():
                m[name] = cur[c] if name == "x_in" else np.ascontiguousarray(fn(c))
            in_maps.append(m)
        res = run_bass_kernel_spmd(nc, in_maps, core_ids=list(range(NCORES)))
        cur = [np.asarray(res.results[c]["x_out"], dtype=np.float32)
               for c in range(NCORES)]
        _last_results = res.results
    return np.concatenate(cur, 0).reshape(2, 8192, 1024)
